# version 02 (final)
# baseline (speedup 1.0000x reference)
import contextlib
import numpy as np

import concourse.bass as bass
import concourse.mybir as mybir
from concourse.bass_utils import run_bass_kernel_spmd

F32 = mybir.dt.float32
BF16 = mybir.dt.bfloat16
I32 = mybir.dt.int32
AF = mybir.ActivationFunctionType
ALU = mybir.AluOpType

BATCH = 8
L = 2048
D = 1024
DI = 2048
H = 32
NST = 64
PH = 64
NPAIR = 16
DPROJ = 4256
T = 128
NC = L // T
KC = D // 128
KI = DI // 128

TWO_PI = float(2 * np.pi)

DBG_HOOK = None

ENGINES = ("tensor", "vector", "scalar", "gpsimd", "sync")
ESEM = {"tensor": "sT", "vector": "sV", "scalar": "sS", "gpsimd": "sG",
        "sync": "sY"}


class Bld:
    NRING = 8

    def __init__(self):
        self.q = {e: [] for e in ENGINES}
        self.cnt = {}
        self.waited = {}
        self.sem_handles = {}
        self.rings = {}

    def new_sem(self, name):
        self.cnt[name] = 0

    def new_ring(self, queue, n=None):
        n = n or self.NRING
        names = [f"{queue}_{i}" for i in range(n)]
        for nm in names:
            self.cnt[nm] = 0
        self.rings[queue] = [names, 0]

    def emit(self, eng, fn):
        self.q[eng].append(fn)
        self.last_instr = (eng, len(self.q[eng]) - 1)

    def wait(self, eng, sem, val):
        if val is None or val <= 0:
            return
        key = (eng, sem)
        if self.waited.get(key, 0) >= val:
            return
        self.waited[key] = val
        h = self.sem_handles
        fn = lambda e, s=sem, v=val: e.wait_ge(h[s], v)
        fn._is_wait = True
        self.emit(eng, fn)

    def wait_ev(self, eng, ev):
        if ev is None:
            return
        if isinstance(ev, tuple):
            ev = [ev]
        for s, v in ev:
            self.wait(eng, s, v)

    def queue_barrier(self, eng, queue):
        for nm in self.rings[queue][0]:
            if self.cnt[nm] > 0:
                self.wait(eng, nm, self.cnt[nm])

    def queue_ev(self, queue):
        return [(nm, self.cnt[nm]) for nm in self.rings[queue][0]
                if self.cnt[nm] > 0]

    def mark(self, eng, sem=None):
        sem = sem or ESEM[eng]
        self.cnt[sem] += 1
        v = self.cnt[sem]
        h = self.sem_handles
        q = self.q[eng]
        if q and getattr(q[-1], "_is_wait", False) is False:
            prev = q[-1]
            def wrapped(e, p=prev, s=sem):
                inst = p(e)
                if inst is not None and hasattr(inst, "then_inc"):
                    inst.then_inc(h[s], 1)
                else:
                    e.sem_inc(h[s], 1)
            q[-1] = wrapped
        else:
            self.emit(eng, lambda e, s=sem: e.sem_inc(h[s], 1))
        return (sem, v)

    def dma(self, eng, queue, out_ap, in_ap):
        names, idx = self.rings[queue]
        sem = names[idx]
        self.rings[queue][1] = (idx + 1) % len(names)
        if self.cnt[sem] > 0:
            self.wait(eng, sem, self.cnt[sem])
        self.cnt[sem] += 16
        v = self.cnt[sem]
        h = self.sem_handles
        self.emit(eng, lambda e, s=sem, o=out_ap, i=in_ap:
                  e.dma_start(o, i).then_inc(h[s], 16))
        return (sem, v)


def ap(t, offset, dims):
    return bass.AP(t, offset, dims)


def build_program():
    nc = bass.Bass("TRN2", target_bir_lowering=False)
    nc.detect_race_conditions = False
    b = Bld()
    for s in ("sT", "sV", "sS", "sG", "sY"):
        b.new_sem(s)
    b.new_ring("sD")
    b.new_ring("sGD")
    b.new_ring("sO", 4)

    x_hbm = nc.dram_tensor("x", [L, D], F32, kind="ExternalInput").ap()
    w_in_hbm = nc.dram_tensor("w_in", [D, DPROJ], F32, kind="ExternalInput").ap()
    w_out_hbm = nc.dram_tensor("w_out", [DI, D], F32, kind="ExternalInput").ap()
    wr_hbm = nc.dram_tensor("w_r", [D, D], F32, kind="ExternalInput").ap()
    wk_hbm = nc.dram_tensor("w_k", [D, D], F32, kind="ExternalInput").ap()
    wv_hbm = nc.dram_tensor("w_v", [D, D], F32, kind="ExternalInput").ap()
    alog_rep_hbm = nc.dram_tensor("i_alog_rep", [128, H], F32, kind="ExternalInput").ap()
    dtb_rep_hbm = nc.dram_tensor("i_dtb_rep", [128, H], F32, kind="ExternalInput").ap()
    dsk_rep_hbm = nc.dram_tensor("i_dsk_rep", [128, DI], F32, kind="ExternalInput").ap()
    nw_cols_hbm = nc.dram_tensor("i_nw_cols", [128, KI], F32, kind="ExternalInput").ap()
    mur_cols_hbm = nc.dram_tensor("i_mur_cols", [128, KC], F32, kind="ExternalInput").ap()
    muk_cols_hbm = nc.dram_tensor("i_muk_cols", [128, KC], F32, kind="ExternalInput").ap()
    identb_hbm = nc.dram_tensor("i_identb", [128, 128], BF16, kind="ExternalInput").ap()
    identf_hbm = nc.dram_tensor("i_identf", [128, 128], F32, kind="ExternalInput").ap()
    if2pi_hbm = nc.dram_tensor("i_if2pi", [128, H * NPAIR], F32, kind="ExternalInput").ap()
    idrep_hbm = nc.dram_tensor("i_idrep", [H, H * T], BF16, kind="ExternalInput").ap()
    wmaskneg_hbm = nc.dram_tensor("i_wmaskneg", [128, 4 * T], BF16, kind="ExternalInput").ap()

    out_hbm = nc.dram_tensor("out", [L, D], F32, kind="ExternalOutput").ap()
    woutb_hbm = nc.dram_tensor("woutb_cache", [DI, D], BF16).ap()
    wrTc_hbm = nc.dram_tensor("wrT_cache", [D, D], BF16).ap()
    wkTc_hbm = nc.dram_tensor("wkT_cache", [D, D], BF16).ap()
    wvTc_hbm = nc.dram_tensor("wvT_cache", [D, D], BF16).ap()

    st = contextlib.ExitStack()

    def sb(name, shape, dtype):
        return st.enter_context(nc.sbuf_tensor(name, shape, dtype))

    def psm(name, shape, dtype=F32):
        return st.enter_context(nc.psum_tensor(name, shape, dtype))

    with st:
        PA0 = psm("PA0", [128, 512])
        PA1 = psm("PA1", [128, 512])
        PB = psm("PB", [128, 1024])
        PC = psm("PC", [128, 512])
        PD = psm("PD", [128, 512])
        PG = psm("PG", [128, 512])
        PF = psm("PF", [128, 512])

        winb = [sb(f"winb{k}", [128, 4224], BF16) for k in range(KC)]
        wout_s = [sb(f"wouts{p}", [128, D], BF16) for p in range(2)]
        wdt = sb("wdt_w", [128, KC * H], F32)

        identb = sb("identb", [128, 128], BF16)
        identf = sb("identf", [128, 128], F32)
        if2pi = sb("if2pi", [128, H * NPAIR], F32)
        dskR = sb("dskR", [128, DI], BF16)
        alogR = sb("alogR", [128, H], F32)
        dtbR = sb("dtbR", [128, H], F32)
        aRep = sb("aRep", [128, H], F32)
        a_col = sb("a_col", [H, 1], F32)
        nega_col = sb("nega_col", [H, 1], F32)
        nw_cols = sb("nw_cols", [128, KI], F32)
        mur_cols = sb("murm1_cols", [128, KC], F32)
        muk_cols = sb("mukm1_cols", [128, KC], F32)
        csRep = sb("csRep", [128, H], F32)
        allonef = sb("allonef", [128, 128], F32)
        e0b = sb("e0b", [66, 128], BF16)
        wmaskneg = sb("wmaskneg", [128, 4 * T], BF16)
        lamrhs = sb("lamrhs", [66, H], BF16)
        c_eps = sb("c_eps", [128, 1], F32)
        c_invD = sb("c_invD", [128, 1], F32)
        vfil = sb("vfil", [128, 1], F32)
        sfil = sb("sfil", [128, 1], F32)
        c_nhalf = sb("c_nhalf", [128, 1], F32)
        c_2pi = sb("c_2pi", [128, 1], F32)
        c_one = sb("c_one", [128, 1], F32)
        zero32 = sb("zero32", [H, 1], F32)

        dtf = sb("dtf", [H, L], F32)
        Df = sb("Df", [H, L], F32)
        rstd_g = sb("rstd_g", [128, NC], F32)
        negmr_g = sb("negmr_g", [128, NC], F32)
        endtok = sb("endtok", [48, H], F32)
        endfm = sb("endfm", [128, 128], F32)
        Hbar = sb("Hbar", [128, H * PH], BF16)

        UL = sb("UL", [128, T], F32)
        WL = sb("WL", [128, T], F32)
        DTL = sb("DTL", [128, T], F32)
        UR = sb("UR", [128, H], F32)
        WR = sb("WR", [128, H], F32)
        ematL = sb("ematL", [66, T], BF16)
        ematR = sb("ematR", [66, H * T], BF16)
        lndt_c = sb("lndt_c", [H, T], F32)
        ndl_f = sb("ndl_f", [H, T], F32)
        ndl_hi = sb("ndl_hi", [H, T], BF16)
        ndl_lo = sb("ndl_lo", [H, T], BF16)
        aprevc = sb("aprevc", [H, 1], F32)
        u_tok = sb("u_tok", [128, H], F32)
        w_tok = sb("w_tok", [128, H], F32)
        dt_tok = sb("dt_tok", [128, H], F32)
        wdt_tok = sb("wdt_tok", [128, H], F32)
        lamrep = sb("lamrep", [128, H], F32)

        xC = sb("xC", [128, D], F32)
        xfeat = sb("xfeat", [128, KC * T], F32)
        xn_tok = sb("xn_tok", [128, D], BF16)
        xnf = sb("xnf", [128, KC * T], BF16)
        xn_last = sb("xn_last", [128, KC], BF16)
        ln_sum = sb("ln_sum", [128, 1], F32)
        ln_sq = sb("ln_sq", [128, 1], F32)
        ln_mean = sb("ln_mean", [128, 1], F32)
        ln_ms = sb("ln_ms", [128, 1], F32)
        ln_var = sb("ln_var", [128, 1], F32)
        ln_rstd = sb("ln_rstd", [128, 1], F32)
        ln_negmr = sb("ln_negmr", [128, 1], F32)
        dtraw1 = sb("dtraw1", [128, 128], F32)
        siluz = sb("siluz", [128, DI], BF16)
        xs_tok = sb("xs_tok", [128, DI], BF16)
        Bc = sb("Bc", [128, NST], F32)
        Cc = sb("Cc", [128, NST], F32)
        p1b = sb("p1b", [128, H * NPAIR], F32)
        kfb = sb("kfb", [128, H * NPAIR], F32)
        psb = sb("psb", [128, H * NPAIR], F32)
        sinb = sb("sinb", [128, H * NPAIR], BF16)
        cosb = sb("cosb", [128, H * NPAIR], BF16)
        s1b = sb("s1b", [128, H * NPAIR], BF16)
        s2b = sb("s2b", [128, H * NPAIR], BF16)
        btok = sb("btok", [128, H * NST], BF16)
        ctok = sb("ctok", [128, H * NST], BF16)
        bwtok = sb("bwtok", [128, H * NST], BF16)
        bfeat = sb("bfeat", [64, H * T], BF16)
        cfeat = sb("cfeat", [64, H * T], BF16)
        cwfeat = sb("cwfeat", [128, H * T], BF16)
        wdec = sb("wdec", [128, 4 * T], BF16)
        gs_b = sb("gs_b", [128, 4 * T], BF16)
        m_b = sb("m_b", [128, 4 * T], BF16)
        y_tok = sb("y_tok", [128, DI], F32)
        yg = sb("yg", [128, DI], BF16)
        ynf = sb("ynf", [128, DI], BF16)
        rms_ss = sb("rms_ss", [128, 1], F32)
        rms_rs = sb("rms_rs", [128, 1], F32)
        xd_f = sb("xd_f", [128, KC * T], BF16)
        xr_f = sb("xr_f", [128, KC * T], BF16)
        xk_f = sb("xk_f", [128, KC * T], BF16)


        for s in ("sT", "sV", "sS", "sG", "sY"):
            b.sem_handles[s] = st.enter_context(nc.semaphore(s))
        for names, _ in b.rings.values():
            for nm in names:
                b.sem_handles[nm] = st.enter_context(nc.semaphore(nm))

        block = st.enter_context(nc.Block())

        def mm(out_ap, lhs_ap, rhs_ap, start, stop, skip_gc=False):
            b.emit("tensor", lambda e, o=out_ap, l=lhs_ap, r=rhs_ap,
                   s1=start, s2=stop, sg=skip_gc: e.matmul(
                       o, l, r, start=s1, stop=s2, skip_group_check=sg))

        def vop(fn):
            b.emit("vector", fn)

        def sop(fn):
            b.emit("scalar", fn)

        def gop(fn):
            b.emit("gpsimd", fn)

        def vsp():
            vop(lambda e: e.tensor_scalar(out=vfil[:, :], in0=vfil[:, :],
                                          scalar1=1.0, scalar2=None,
                                          op0=ALU.mult))

        def ssp():
            sop(lambda e: e.activation(out=sfil[:, :], in_=c_one[:, :],
                                       func=AF.Copy))

        def dbg(c, name, view):
            if DBG_HOOK is not None:
                DBG_HOOK(nc, b, c, name, view)

        for dst, src in [(identb, identb_hbm), (identf, identf_hbm),
                         (if2pi, if2pi_hbm), (alogR, alog_rep_hbm),
                         (dtbR, dtb_rep_hbm), (nw_cols, nw_cols_hbm),
                         (mur_cols, mur_cols_hbm), (muk_cols, muk_cols_hbm),
                         (wmaskneg, wmaskneg_hbm)]:
            b.dma("sync", "sD", dst[:, :], src)
        ev_cdma = b.queue_ev("sD")
        b.dma("sync", "sD", ematR[0:H, :], idrep_hbm)
        b.dma("sync", "sD", ematR[H:2 * H, :], idrep_hbm)
        for k in range(KC):
            b.dma("sync", "sD", wdt[:, k * H:(k + 1) * H],
                  w_in_hbm[k * 128:(k + 1) * 128, 2 * DI + 2 * NST:DPROJ])
        ev_wdt = b.queue_ev("sD")

        def gset(apv, val):
            gop(lambda e, a=apv, v=val: e.memset(a, v))
        gset(c_eps[:, :], 1e-5)
        gset(c_invD[:, :], 1.0 / D)
        gset(vfil[:, :], 1.0)
        gset(c_nhalf[:, :], -0.5)
        gset(c_2pi[:, :], TWO_PI)
        gset(c_one[:, :], 1.0)
        gset(zero32[:, :], 0.0)
        gset(allonef[:, :], 1.0)
        gset(Hbar[:, :], 0.0)
        gset(UL[:, :], 0.0)
        gset(WL[:, :], 0.0)
        gset(DTL[:, :], 0.0)
        gset(UR[:, :], 0.0)
        gset(WR[:, :], 0.0)
        gset(ematL[:, :], 0.0)
        gset(cwfeat[64:128, :], 0.0)
        gset(e0b[:, :], 0.0)
        gset(lamrhs[:, :], 0.0)
        gset(endfm[:, :], 0.0)
        gset(dtraw1[:, :], 0.0)
        gset(dtf[:, :], 0.0)
        ev_mset0 = b.mark("gpsimd")
        b.wait_ev("gpsimd", ev_mset0)
        gset(UL[32:33, :], 1.0)
        gset(WL[32:33, :], 1.0)
        gset(ematL[64:66, :], 1.0)
        gset(e0b[0:1, :], 1.0)
        ev_mset = b.mark("gpsimd")

        b.wait_ev("gpsimd", ev_cdma)
        b.dma("gpsimd", "sGD", UR[0:H, :], identf[0:H, 0:H])
        b.dma("gpsimd", "sGD", WR[0:H, :], identf[0:H, 0:H])
        b.dma("gpsimd", "sGD", dskR[:, :], dsk_rep_hbm)
        for k in range(KC):
            b.dma("gpsimd", "sGD", winb[k][:, :],
                  w_in_hbm[k * 128:(k + 1) * 128, 0:4224])
        ev_winb = b.queue_ev("sGD")

        b.wait_ev("scalar", ev_cdma)
        sop(lambda e: e.activation(out=aRep[:, :], in_=alogR[:, :], func=AF.Exp))
        ssp()
        ssp()
        ev_arep = b.mark("scalar")
        b.wait_ev("gpsimd", ev_arep)
        ev_acol = b.dma("gpsimd", "sGD", a_col[0:H, 0:1], aRep[0:1, 0:H])
        b.wait_ev("vector", ev_acol)
        vop(lambda e: e.tensor_scalar(out=nega_col[0:H, 0:1], in0=a_col[0:H, 0:1],
                                      scalar1=-1.0, scalar2=None, op0=ALU.mult))
        b.wait_ev("vector", ev_cdma)
        vop(lambda e: e.tensor_scalar(out=mur_cols[:, :], in0=mur_cols[:, :],
                                      scalar1=-1.0, scalar2=None, op0=ALU.add))
        vop(lambda e: e.tensor_scalar(out=muk_cols[:, :], in0=muk_cols[:, :],
                                      scalar1=-1.0, scalar2=None, op0=ALU.add))
        ev_mu = b.mark("vector")

        b.wait_ev("tensor", ev_wdt)
        b.wait_ev("tensor", ev_mset)
        for k in range(KC):
            mm(PF[:, 0:H], allonef[:, :], wdt[:, k * H:(k + 1) * H],
               k == 0, k == KC - 1)
        ev_csmm = b.mark("tensor")
        b.wait_ev("vector", ev_csmm)
        vop(lambda e: e.tensor_copy(out=csRep[:, :], in_=PF[:, 0:H]))
        ev_csrep = b.mark("vector")
        b.wait_ev("tensor", ev_csrep)

        wT_marks = []
        wT_dmas = []
        idx = 0
        for src_h, cacheT, swapT in [(wr_hbm, wrTc_hbm, False),
                                     (wk_hbm, wkTc_hbm, True),
                                     (wv_hbm, wvTc_hbm, False)]:
            for dblk in range(KC):
                if idx >= 1:
                    b.wait_ev("gpsimd", wT_marks[idx - 1])
                evw = b.dma("gpsimd", "sGD", wout_s[1][:, :],
                            src_h[dblk * 128:(dblk + 1) * 128, :])
                b.wait_ev("tensor", evw)
                b.wait_ev("tensor", ev_cdma)
                if idx >= 1:
                    b.wait_ev("tensor", wT_marks[idx - 1])
                    b.wait_ev("tensor", wT_dmas[-KC:])
                for cb in range(KC):
                    pa = PA0 if cb < 4 else PA1
                    mm(pa[:, (cb % 4) * 128:(cb % 4 + 1) * 128],
                       wout_s[1][:, cb * 128:(cb + 1) * 128], identb[:, :],
                       True, True)
                evm = b.mark("tensor")
                b.wait_ev("vector", evm)
                b.wait_ev("vector", wT_dmas[-KC:] if wT_dmas else None)
                vop(lambda e: e.tensor_copy(out=wout_s[0][:, 0:512], in_=PA0[:, :]))
                vop(lambda e: e.tensor_copy(out=wout_s[0][:, 512:1024], in_=PA1[:, :]))
                wT_marks.append(b.mark("vector"))
                b.wait_ev("gpsimd", wT_marks[-1])
                for cb in range(KC):
                    if swapT:
                        dst = cacheT[dblk * 128:(dblk + 1) * 128,
                                     cb * 128:(cb + 1) * 128]
                    else:
                        dst = cacheT[cb * 128:(cb + 1) * 128,
                                     dblk * 128:(dblk + 1) * 128]
                    wT_dmas.append(b.dma(
                        "gpsimd", "sGD", dst,
                        wout_s[0][:, cb * 128:(cb + 1) * 128]))
                idx += 1
        ev_ffnT = wT_marks[-1]
        ev_ffnTd = wT_dmas[-KC:]
        b.wait_ev("tensor", ev_ffnT)

        woutb_evs = []
        for kk in range(KI):
            par = kk % 2
            if kk < 2:
                b.wait_ev("gpsimd", ev_ffnTd)
                b.wait_ev("gpsimd", ev_ffnT)
            if kk >= 2:
                b.wait_ev("gpsimd", woutb_evs[kk - 2])
            evw = b.dma("gpsimd", "sGD", wout_s[par][:, :],
                        w_out_hbm[kk * 128:(kk + 1) * 128, :])
            b.wait_ev("vector", evw)
            vop(lambda e, par=par, kk=kk: e.tensor_scalar(
                out=wout_s[par][:, :], in0=wout_s[par][:, :],
                scalar1=nw_cols[:, kk:kk + 1], scalar2=None, op0=ALU.mult))
            evv = b.mark("vector")
            b.wait_ev("gpsimd", evv)
            woutb_evs.append(b.dma("gpsimd", "sGD",
                                   woutb_hbm[kk * 128:(kk + 1) * 128, :],
                                   wout_s[par][:, :]))
        ev_woutb = woutb_evs[-1]

        ev_dtevac = None
        ev_xuseT = None
        ev_xuseS = None
        ev_statsfree = None
        for c in range(NC):
            b.wait_ev("sync", ev_xuseT)
            b.wait_ev("sync", ev_xuseS)
            evx = b.dma("sync", "sD", xC[:, :], x_hbm[c * T:(c + 1) * T, :])
            b.wait_ev("scalar", evx)
            b.wait_ev("scalar", ev_mset)
            b.wait_ev("scalar", ev_statsfree)
            sop(lambda e: e.activation(out=xfeat[:, :], in_=xC[:, :],
                                       func=AF.Copy, accum_out=ln_sum[:, :]))
            sop(lambda e: e.activation(out=y_tok[:, 0:D], in_=xC[:, :],
                                       func=AF.Square, accum_out=ln_sq[:, :]))
            ev_xuseS = b.mark("scalar")
            b.wait_ev("vector", ev_xuseS)
            vop(lambda e: e.tensor_tensor(out=ln_ms[:, :], in0=ln_sum[:, :],
                                          in1=ln_sum[:, :], op=ALU.mult))
            vop(lambda e: e.tensor_scalar(out=ln_mean[:, :], in0=ln_sum[:, :],
                                          scalar1=1.0 / D, scalar2=None,
                                          op0=ALU.mult))
            vop(lambda e: e.scalar_tensor_tensor(out=ln_var[:, :], in0=ln_ms[:, :],
                                                 scalar=-1.0 / D, in1=ln_sq[:, :],
                                                 op0=ALU.mult, op1=ALU.add))
            vsp()
            evv = b.mark("vector")
            ev_statsfree = evv
            b.wait_ev("scalar", evv)
            sop(lambda e: e.activation(out=ln_rstd[:, :], in_=ln_var[:, :],
                                       func=AF.Ln, scale=c_invD[:, :],
                                       bias=c_eps[:, :]))
            ssp()
            sop(lambda e: e.activation(out=ln_rstd[:, :], in_=ln_rstd[:, :],
                                       func=AF.Exp, scale=c_nhalf[:, :]))
            ssp()
            ssp()
            evs2 = b.mark("scalar")
            b.wait_ev("vector", evs2)
            vop(lambda e: e.scalar_tensor_tensor(out=ln_negmr[:, :], in0=ln_mean[:, :],
                                                 scalar=-1.0, in1=ln_rstd[:, :],
                                                 op0=ALU.mult, op1=ALU.mult))
            vop(lambda e, c=c: e.tensor_copy(out=rstd_g[:, c:c + 1], in_=ln_rstd[:, :]))
            vop(lambda e, c=c: e.tensor_copy(out=negmr_g[:, c:c + 1], in_=ln_negmr[:, :]))
            ev_lncols = b.mark("vector")

            b.wait_ev("tensor", evx)
            b.wait_ev("tensor", ev_mset)
            b.wait_ev("tensor", ev_dtevac)
            for kb in range(KC):
                mm(PF[:, 128:256], xC[:, kb * 128:(kb + 1) * 128], identf[:, :],
                   True, True)
                evt = b.mark("tensor")
                b.wait_ev("vector", evt)
                b.wait_ev("vector", ev_xuseS)
                vop(lambda e, kb=kb: e.tensor_copy(
                    out=xfeat[:, kb * T:(kb + 1) * T], in_=PF[:, 128:256]))
                evf = b.mark("vector")
                b.wait_ev("tensor", evf)
            dbg(c, "xfeatA", xfeat[:, :])
            for kb in range(KC):
                mm(PF[:, 256:256 + H], xfeat[:, kb * T:(kb + 1) * T],
                   wdt[:, kb * H:(kb + 1) * H], kb == 0, kb == KC - 1)
            ev_xuseT = b.mark("tensor")
            b.wait_ev("vector", ev_xuseT)
            b.wait_ev("vector", ev_csrep)
            b.wait_ev("vector", ev_lncols)
            vop(lambda e: e.scalar_tensor_tensor(out=xfeat[:, 0:H],
                                                 in0=csRep[:, :],
                                                 scalar=ln_negmr[:, 0:1],
                                                 in1=dtbR[:, :],
                                                 op0=ALU.mult, op1=ALU.add))
            vop(lambda e: e.tensor_scalar(out=dtraw1[:, 0:H], in0=PF[:, 256:256 + H],
                                          scalar1=ln_rstd[:, 0:1], scalar2=None,
                                          op0=ALU.mult))
            vsp()
            vop(lambda e: e.tensor_tensor(out=dtraw1[:, 0:H], in0=dtraw1[:, 0:H],
                                          in1=xfeat[:, 0:H], op=ALU.add))
            vsp()
            evd1 = b.mark("vector")
            dbg(c, "dtrawA", dtraw1[:, 0:H])
            if DBG_HOOK is not None:
                vop(lambda e: e.tensor_copy(out=y_tok[:, 256:256 + H],
                                            in_=PF[:, 256:256 + H]))
                b.mark("vector")
                dbg(c, "pfdt", y_tok[:, 256:256 + H])
            dbg(c, "csrep", csRep[:, :])
            dbg(c, "lnr", ln_rstd[:, :])
            dbg(c, "dtbr", dtbR[:, :])
            dbg(c, "lnsum", ln_sum[:, :])
            dbg(c, "lnsq", ln_sq[:, :])
            dbg(c, "lnvar", ln_var[:, :])
            dbg(c, "ceps", c_eps[:, :])
            dbg(c, "cnh", c_nhalf[:, :])
            b.wait_ev("scalar", evd1)
            sop(lambda e: e.activation(out=dtraw1[:, 0:H], in_=dtraw1[:, 0:H],
                                       func=AF.Exp))
            ssp()
            sop(lambda e: e.activation(out=dtraw1[:, 0:H], in_=dtraw1[:, 0:H],
                                       func=AF.Ln, bias=c_one[:, :]))
            ssp()
            ssp()
            evsp = b.mark("scalar")
            b.wait_ev("tensor", evsp)
            mm(PF[:, 384:512], dtraw1[:, :], identf[:, :], True, True)
            evdT = b.mark("tensor")
            b.wait_ev("vector", evdT)
            vop(lambda e, c=c: e.tensor_copy(out=dtf[:, c * T:(c + 1) * T],
                                             in_=PF[0:H, 384:512]))
            ev_dtevac = b.mark("vector")
            dbg(c, "dtf_slice", dtf[0:H, c * T:(c + 1) * T])
            if c == NC - 1:
                dbg(c, "lnstats", rstd_g[:, :])

        vop(lambda e: e.tensor_tensor_scan(out=Df[:, :], data0=dtf[:, :],
                                           data1=dtf[:, :], initial=0.0,
                                           op0=ALU.add, op1=ALU.bypass))
        ev_scan = b.mark("vector")
        dbg(0, "Df", Df[0:H, :])
        dfe = ap(Df, T - 1, [[L, H], [T, NC]])
        vop(lambda e, dfe=dfe: e.tensor_scalar(out=endfm[0:H, 16:16 + NC], in0=dfe,
                                               scalar1=nega_col[0:H, 0:1],
                                               scalar2=None, op0=ALU.mult))
        vop(lambda e, dfe=dfe: e.tensor_scalar(out=endfm[0:H, 0:NC], in0=dfe,
                                               scalar1=a_col[0:H, 0:1],
                                               scalar2=None, op0=ALU.mult))
        vop(lambda e: e.tensor_copy(out=endfm[0:H, 32:33], in_=endfm[0:H, 16:17]))
        if NC > 1:
            vop(lambda e: e.tensor_tensor(out=endfm[0:H, 33:32 + NC],
                                          in0=endfm[0:H, 17:16 + NC],
                                          in1=endfm[0:H, 16:15 + NC],
                                          op=ALU.subtract))
        vsp()
        ev_endf = b.mark("vector")
        b.wait_ev("scalar", ev_endf)
        sop(lambda e: e.activation(out=endfm[0:H, 32:32 + NC],
                                   in_=endfm[0:H, 32:32 + NC], func=AF.Exp))
        ssp()
        ssp()
        ev_lamf = b.mark("scalar")
        b.wait_ev("tensor", ev_lamf)
        mm(PF[:, 384:384 + H], endfm[:, :], identf[:, 0:H], True, True)
        ev_endT = b.mark("tensor")
        b.wait_ev("vector", ev_endT)
        vop(lambda e: e.tensor_copy(out=endtok[:, :], in_=PF[0:48, 384:384 + H]))
        ev_endtok = b.mark("vector")
        dbg(0, "endtok", endtok[:, :])

        ev_fin_prev = ev_endtok
        ev_xuse_prev = ev_xuseT
        ev_pa_free = ev_ffnT
        ev_pf_free = ev_endtok
        ev_out_dma = None
        for c in range(NC):
            par = c % 2
            b.wait_ev("sync", ev_fin_prev)
            b.wait_ev("sync", ev_xuse_prev)
            b.wait_ev("sync", ev_out_dma)
            evx = b.dma("sync", "sD", xC[:, :], x_hbm[c * T:(c + 1) * T, :])
            b.wait_ev("vector", evx)
            vop(lambda e, c=c: e.tensor_scalar(out=xn_tok[:, :], in0=xC[:, :],
                                               scalar1=rstd_g[:, c:c + 1],
                                               scalar2=negmr_g[:, c:c + 1],
                                               op0=ALU.mult, op1=ALU.add))
            ev_xn = b.mark("vector")
            b.wait_ev("tensor", ev_xn)
            b.wait_ev("tensor", ev_pa_free)
            for kb in range(KC):
                pa = PA0 if kb < 4 else PA1
                mm(pa[:, (kb % 4) * 128:(kb % 4 + 1) * 128],
                   xn_tok[:, kb * 128:(kb + 1) * 128], identb[:, :], True, True)
            ev_xnT = b.mark("tensor")
            b.wait_ev("vector", ev_xnT)
            vop(lambda e: e.tensor_copy(out=xnf[:, 0:512], in_=PA0[:, :]))
            vop(lambda e: e.tensor_copy(out=xnf[:, 512:1024], in_=PA1[:, :]))
            ev_xnf = b.mark("vector")
            dbg(c, "xnf", xnf[:, :])

            b.wait_ev("vector", ev_scan)
            vop(lambda e, c=c: e.tensor_scalar(
                out=UL[0:H, :], in0=Df[0:H, c * T:(c + 1) * T],
                scalar1=nega_col[0:H, 0:1], scalar2=None, op0=ALU.mult))
            vop(lambda e, c=c: e.tensor_scalar(
                out=WL[0:H, :], in0=Df[0:H, c * T:(c + 1) * T],
                scalar1=a_col[0:H, 0:1], scalar2=None, op0=ALU.mult))
            vop(lambda e, c=c: e.tensor_copy(out=DTL[0:H, :],
                                             in_=Df[0:H, c * T:(c + 1) * T]))
            if c == 0:
                vop(lambda e: e.tensor_copy(out=aprevc[:, :], in_=zero32[:, :]))
            else:
                vop(lambda e, c=c: e.tensor_scalar(
                    out=aprevc[:, :], in0=Df[0:H, c * T - 1:c * T],
                    scalar1=a_col[0:H, 0:1], scalar2=None, op0=ALU.mult))
            ev_dtv = b.mark("vector")
            b.wait_ev("scalar", ev_dtv)
            sop(lambda e, c=c: e.activation(out=lndt_c[:, :],
                                            in_=dtf[0:H, c * T:(c + 1) * T],
                                            func=AF.Ln))
            ssp()
            ev_lndt = b.mark("scalar")
            b.wait_ev("vector", ev_lndt)
            vop(lambda e: e.scalar_tensor_tensor(out=ndl_f[:, :], in0=WL[0:H, :],
                                                 scalar=aprevc[0:H, 0:1],
                                                 in1=lndt_c[:, :],
                                                 op0=ALU.subtract, op1=ALU.add))
            vop(lambda e: e.tensor_copy(out=ematL[0:H, :], in_=ndl_f[:, :]))
            vsp()
            vop(lambda e: e.tensor_tensor(out=ematL[H:2 * H, :],
                                          in0=ndl_f[:, :], in1=ematL[0:H, :],
                                          op=ALU.subtract))
            vop(lambda e: e.tensor_scalar(out=ndl_f[:, :], in0=UL[0:H, :],
                                          scalar1=aprevc[0:H, 0:1], scalar2=None,
                                          op0=ALU.add))
            vop(lambda e: e.tensor_copy(out=ndl_hi[:, :], in_=ndl_f[:, :]))
            vsp()
            vop(lambda e: e.tensor_tensor(out=ndl_lo[:, :], in0=ndl_f[:, :],
                                          in1=ndl_hi[:, :], op=ALU.subtract))
            vsp()
            ev_ndl = b.mark("vector")
            b.wait_ev("gpsimd", ev_ndl)
            b.wait_ev("gpsimd", ev_endtok)
            b.dma("gpsimd", "sGD", ematR[64:65, :], ndl_hi[:, :])
            b.dma("gpsimd", "sGD", ematR[65:66, :], ndl_lo[:, :])
            if c > 0:
                b.dma("gpsimd", "sGD", UR[32:33, :], endtok[c - 1:c, :])
            b.dma("gpsimd", "sGD", WR[32:33, :], endtok[16 + c:17 + c, :])
            b.dma("gpsimd", "sGD", lamrhs[0:1, :], endtok[32 + c:33 + c, :])
            ev_rows = b.queue_ev("sGD")

            b.wait_ev("tensor", ev_ndl)
            b.wait_ev("tensor", ev_rows)
            b.wait_ev("tensor", ev_pf_free)
            mm(PF[:, 0:H], DTL[:, :], identf[:, 0:H], True, True)
            mm(PF[:, 64:64 + H], UL[:, :], UR[:, 0:H], True, True)
            mm(PF[:, 128:128 + H], WL[:, :], WR[:, 0:H], True, True)
            ev_sm1 = b.mark("tensor")
            b.wait_ev("vector", ev_sm1)
            vop(lambda e, c=c: e.tensor_copy(out=DTL[0:H, :],
                                             in_=dtf[0:H, c * T:(c + 1) * T]))
            ev_dtl2 = b.mark("vector")
            b.wait_ev("tensor", ev_dtl2)
            mm(PF[:, 192:192 + H], DTL[:, :], identf[:, 0:H], True, True)
            ev_sm2 = b.mark("tensor")

            b.wait_ev("scalar", ev_sm2)
            sop(lambda e: e.activation(out=u_tok[:, :], in_=PF[:, 64:64 + H],
                                       func=AF.Exp))
            sop(lambda e: e.activation(out=w_tok[:, :], in_=PF[:, 128:128 + H],
                                       func=AF.Exp))
            ssp()
            ev_uw = b.mark("scalar")
            b.wait_ev("vector", ev_sm2)
            vop(lambda e: e.tensor_copy(out=dt_tok[:, :], in_=PF[:, 192:192 + H]))
            b.wait_ev("vector", ev_uw)
            vop(lambda e: e.tensor_tensor(out=wdt_tok[:, :], in0=w_tok[:, :],
                                          in1=dt_tok[:, :], op=ALU.mult))
            dtok_ap = ap(PF, 0, [[512, 128], [1, H], [0, NPAIR]])
            vop(lambda e, a_=dtok_ap: e.tensor_tensor(out=p1b[:, :], in0=a_,
                                                      in1=if2pi[:, :], op=ALU.mult))
            psb_i = psb[:, :].bitcast(I32)
            vop(lambda e, a_=psb_i: e.tensor_copy(out=a_, in_=p1b[:, :]))
            vop(lambda e, a_=psb_i: e.tensor_copy(out=kfb[:, :], in_=a_))
            vop(lambda e: e.tensor_tensor(out=psb[:, :], in0=p1b[:, :],
                                          in1=kfb[:, :], op=ALU.subtract))
            vop(lambda e: e.tensor_scalar(out=kfb[:, :], in0=psb[:, :],
                                          scalar1=0.5, scalar2=None,
                                          op0=ALU.is_gt))
            vop(lambda e: e.tensor_tensor(out=psb[:, :], in0=psb[:, :],
                                          in1=kfb[:, :], op=ALU.subtract))
            ev_psi = b.mark("vector")
            ev_pf_free = ev_psi
            b.wait_ev("scalar", ev_psi)
            sop(lambda e: e.activation(out=sinb[:, :], in_=psb[:, :], func=AF.Sin,
                                       scale=c_2pi[:, :]))
            ev_sin = b.mark("scalar")
            b.wait_ev("vector", ev_sin)
            vop(lambda e, a_=psb_i: e.tensor_scalar(out=a_, in0=p1b[:, :],
                                                    scalar1=0.25, scalar2=None,
                                                    op0=ALU.add))
            vop(lambda e, a_=psb_i: e.tensor_copy(out=kfb[:, :], in_=a_))
            vop(lambda e: e.scalar_tensor_tensor(out=psb[:, :], in0=p1b[:, :],
                                                 scalar=0.25, in1=kfb[:, :],
                                                 op0=ALU.add, op1=ALU.subtract))
            vop(lambda e: e.tensor_scalar(out=kfb[:, :], in0=psb[:, :],
                                          scalar1=0.5, scalar2=None,
                                          op0=ALU.is_gt))
            vop(lambda e: e.tensor_tensor(out=psb[:, :], in0=psb[:, :],
                                          in1=kfb[:, :], op=ALU.subtract))
            ev_psc = b.mark("vector")
            b.wait_ev("scalar", ev_psc)
            sop(lambda e: e.activation(out=cosb[:, :], in_=psb[:, :], func=AF.Sin,
                                       scale=c_2pi[:, :]))
            ev_cos = b.mark("scalar")
            dbg(c, "u_tok", u_tok[:, :])
            dbg(c, "wdt_tok", wdt_tok[:, :])
            dbg(c, "sinb", sinb[:, :])
            dbg(c, "cosb", cosb[:, :])

            b.wait_ev("tensor", ev_xnf)
            b.wait_ev("tensor", ev_winb)
            groups = ([("z", g, g * 512, 512) for g in range(4)]
                      + [("xs", g, DI + g * 512, 512) for g in range(4)]
                      + [("bc", 0, 2 * DI, 128)])
            prev_evac = [None, None]
            for gi, (kind, g, col0, ncol) in enumerate(groups):
                half = gi % 2
                b.wait_ev("tensor", prev_evac[half])
                pa = PA0 if half == 0 else PA1
                for kb in range(KC):
                    mm(pa[:, 0:ncol],
                       xnf[:, kb * T:(kb + 1) * T],
                       winb[kb][:, col0:col0 + ncol], kb == 0, kb == KC - 1)
                evg = b.mark("tensor")
                if kind == "z":
                    b.wait_ev("scalar", evg)
                    sop(lambda e, g=g, pa=pa: e.activation(
                        out=siluz[:, g * 512:(g + 1) * 512],
                        in_=pa[:, 0:512], func=AF.Silu))
                    prev_evac[half] = b.mark("scalar")
                elif kind == "xs":
                    b.wait_ev("vector", evg)
                    vop(lambda e, g=g, pa=pa: e.tensor_copy(
                        out=xs_tok[:, g * 512:(g + 1) * 512],
                        in_=pa[:, 0:512]))
                    prev_evac[half] = b.mark("vector")
                else:
                    b.wait_ev("vector", evg)
                    vop(lambda e, pa=pa: e.tensor_copy(
                        out=Bc[:, :], in_=pa[:, 0:NST]))
                    vop(lambda e, pa=pa: e.tensor_copy(
                        out=Cc[:, :], in_=pa[:, NST:2 * NST]))
                    prev_evac[half] = b.mark("vector")
            ev_pa_free = [x for x in prev_evac if x is not None]
            ev_inproj_v = ("sV", b.cnt["sV"])
            ev_inproj_s = ("sS", b.cnt["sS"])
            dbg(c, "siluz", siluz[:, :])
            dbg(c, "xs_tok", xs_tok[:, :])
            dbg(c, "Bc", Bc[:, :])
            dbg(c, "Cc", Cc[:, :])

            b.wait_ev("vector", ev_inproj_v)
            vop(lambda e: e.tensor_tensor(out=y_tok[:, :], in0=dskR[:, :],
                                          in1=xs_tok[:, :], op=ALU.mult))
            ev_dxs = b.mark("vector")

            b.wait_ev("vector", ev_cos)
            for (src, dst) in [(Bc, btok), (Cc, ctok)]:
                e_ap = ap(src, 0, [[NST, 128], [0, H], [2, NPAIR]])
                o_ap = ap(src, 1, [[NST, 128], [0, H], [2, NPAIR]])
                u_ap = ap(src, 32, [[NST, 128], [0, H], [1, 32]])
                de_ap = ap(dst, 0, [[H * NST, 128], [NST, H], [1, NPAIR]])
                do_ap = ap(dst, NPAIR, [[H * NST, 128], [NST, H], [1, NPAIR]])
                du_ap = ap(dst, 32, [[H * NST, 128], [NST, H], [1, 32]])
                vop(lambda e, a_=e_ap: e.tensor_tensor(out=s1b[:, :], in0=cosb[:, :],
                                                       in1=a_, op=ALU.mult))
                vop(lambda e, a_=o_ap: e.tensor_tensor(out=s2b[:, :], in0=sinb[:, :],
                                                       in1=a_, op=ALU.mult))
                vop(lambda e, a_=de_ap: e.tensor_tensor(out=a_, in0=s1b[:, :],
                                                        in1=s2b[:, :], op=ALU.add))
                vop(lambda e, a_=o_ap: e.tensor_tensor(out=s1b[:, :], in0=cosb[:, :],
                                                       in1=a_, op=ALU.mult))
                vop(lambda e, a_=e_ap: e.tensor_tensor(out=s2b[:, :], in0=sinb[:, :],
                                                       in1=a_, op=ALU.mult))
                vop(lambda e, a_=do_ap: e.tensor_tensor(out=a_, in0=s1b[:, :],
                                                        in1=s2b[:, :], op=ALU.subtract))
                vop(lambda e, a_=u_ap, d_=du_ap: e.tensor_copy(out=d_, in_=a_))
            ev_tok = b.mark("vector")
            dbg(c, "btok", btok[:, :])
            dbg(c, "ctok", ctok[:, :])

            b.wait_ev("tensor", ev_tok)
            for (tok, feat) in [(btok, bfeat), (ctok, cfeat)]:
                for h4 in range(8):
                    for hh in range(4):
                        h = h4 * 4 + hh
                        mm(PD[0:64, hh * 128:(hh + 1) * 128],
                           tok[:, h * NST:(h + 1) * NST], identb[:, :],
                           True, True)
                    evt = b.mark("tensor")
                    b.wait_ev("vector", evt)
                    vop(lambda e, feat=feat, h4=h4: e.tensor_copy(
                        out=feat[0:64, h4 * 512:(h4 + 1) * 512], in_=PD[0:64, :]))
                    evfm = b.mark("vector")
                    b.wait_ev("tensor", evfm)
            u_ap = ap(u_tok, 0, [[H, 128], [1, H], [0, NST]])
            vop(lambda e, a_=u_ap: e.tensor_tensor(out=bwtok[:, :], in0=ctok[:, :],
                                                   in1=a_, op=ALU.mult))
            ev_cw = b.mark("vector")
            b.wait_ev("tensor", ev_cw)
            for h4 in range(8):
                for hh in range(4):
                    h = h4 * 4 + hh
                    mm(PD[0:64, hh * 128:(hh + 1) * 128],
                       bwtok[:, h * NST:(h + 1) * NST], identb[:, :], True, True)
                evt = b.mark("tensor")
                b.wait_ev("vector", evt)
                vop(lambda e, h4=h4: e.tensor_copy(
                    out=cwfeat[0:64, h4 * 512:(h4 + 1) * 512], in_=PD[0:64, :]))
                evfm = b.mark("vector")
                b.wait_ev("tensor", evfm)
            w_ap = ap(wdt_tok, 0, [[H, 128], [1, H], [0, NST]])
            vop(lambda e, a_=w_ap: e.tensor_tensor(out=bwtok[:, :], in0=btok[:, :],
                                                   in1=a_, op=ALU.mult))
            ev_bw = b.mark("vector")
            dbg(c, "bfeat", bfeat[0:64, :])
            dbg(c, "cfeat", cfeat[0:64, :])
            dbg(c, "cwfeat", cwfeat[:, :])
            dbg(c, "bwtok", bwtok[:, :])

            b.wait_ev("tensor", ev_rows)
            b.wait_ev("tensor", ev_mset)
            if c > 0:
                b.wait_ev("tensor", ev_pb_free)
            mm(PB[:, 0:H], e0b[:, :], lamrhs[:, 0:H], True, True)
            ev_lam = b.mark("tensor")
            b.wait_ev("vector", ev_lam)
            vop(lambda e: e.tensor_copy(out=lamrep[:, :], in_=PB[:, 0:H]))
            ev_lamr = b.mark("vector")
            ev_pb_free = ev_lamr

            ev_yevac = None
            ev_gsuse = None
            ev_psb_free = ev_cos
            for g in range(8):
                h0 = g * 4
                b.wait_ev("tensor", ev_gsuse)
                for hh in range(4):
                    h = h0 + hh
                    mm(PG[:, hh * 128:(hh + 1) * 128],
                       ap(bfeat, h * T, [[H * T, 64], [1, T]]),
                       ap(cfeat, h * T, [[H * T, 64], [1, T]]), True, True)
                ev_g = b.mark("tensor")
                b.wait_ev("tensor", ev_pb_free)
                mm(PB[:, 0:512], ematL[:, :], ematR[:, h0 * T:(h0 + 4) * T],
                   True, True)
                ev_em = b.mark("tensor")
                b.wait_ev("vector", ev_em)
                if g > 0:
                    b.wait_ev("vector", ev_psb_free)
                vop(lambda e: e.scalar_tensor_tensor(
                    out=psb[:, :], in0=PB[:, 0:512], scalar=30.0,
                    in1=wmaskneg[:, :], op0=ALU.min, op1=ALU.add))
                ev_wcl = b.mark("vector")
                ev_pb_free = ev_wcl
                b.wait_ev("scalar", ev_wcl)
                if g > 0:
                    b.wait_ev("scalar", ev_m_prev)
                sop(lambda e: e.activation(out=wdec[:, :], in_=psb[:, 0:512],
                                           func=AF.Exp))
                ev_wd = b.mark("scalar")
                ev_psb_free = ev_wd
                if g == 0:
                    dbg(c, "wdec0", wdec[:, :])
                b.wait_ev("scalar", ev_g)
                b.wait_ev("scalar", ev_yevac)
                sop(lambda e: e.activation(out=gs_b[:, :], in_=PG[:, :],
                                           func=AF.Copy))
                ev_gs = b.mark("scalar")
                ev_gsuse = ev_gs
                b.wait_ev("vector", ev_wd)
                b.wait_ev("vector", ev_gs)
                vop(lambda e: e.tensor_tensor(
                    out=m_b[:, :], in0=wdec[:, :],
                    in1=gs_b[:, :], op=ALU.mult))
                ev_m = b.mark("vector")
                ev_m_prev = ev_m
                if g == 0:
                    dbg(c, "gs0", gs_b[:, :])
                    dbg(c, "m0", m_b[:, :])
                b.wait_ev("tensor", ev_m)
                b.wait_ev("tensor", ev_bw)
                b.wait_ev("tensor", ev_yevac)
                b.wait_ev("tensor", ev_inproj_v)
                for hh in range(4):
                    h = h0 + hh
                    mm(PC[:, hh * 128:hh * 128 + PH],
                       cwfeat[:, h * T:(h + 1) * T],
                       Hbar[:, h * PH:(h + 1) * PH], True, False)
                    mm(PC[:, hh * 128:hh * 128 + PH],
                       m_b[:, hh * T:(hh + 1) * T],
                       xs_tok[:, h * PH:(h + 1) * PH], False, True)
                for hh in range(4):
                    h = h0 + hh
                    mm(PD[0:64, hh * PH:(hh + 1) * PH],
                       bwtok[:, h * NST:(h + 1) * NST],
                       xs_tok[:, h * PH:(h + 1) * PH], True, True)
                ev_ys = b.mark("tensor")
                b.wait_ev("vector", ev_ys)
                b.wait_ev("vector", ev_dxs)
                pc_ap = ap(PC, 0, [[512, 128], [128, 4], [1, PH]])
                y_ap = ap(y_tok, h0 * PH, [[DI, 128], [PH, 4], [1, PH]])
                vop(lambda e, a1=pc_ap, a3=y_ap:
                    e.tensor_tensor(out=a3, in0=a3, in1=a1, op=ALU.add))
                b.wait_ev("vector", ev_lamr)
                lam_ap = ap(lamrep, h0, [[H, 64], [1, 4], [0, PH]])
                hb_ap = ap(Hbar, h0 * PH, [[H * PH, 64], [1, 4 * PH]])
                sc_ap = ap(gs_b, 0, [[4 * T, 64], [1, 4 * PH]])
                pd_ap = ap(PD, 0, [[512, 64], [1, 4 * PH]])
                vop(lambda e, a1=hb_ap, a2=lam_ap, a3=sc_ap:
                    e.tensor_tensor(out=a3, in0=a1, in1=a2, op=ALU.mult))
                vop(lambda e, a1=sc_ap, a2=pd_ap, a3=hb_ap:
                    e.tensor_tensor(out=a3, in0=a1, in1=a2, op=ALU.add))
                ev_yevac = b.mark("vector")
                if g == 0:
                    dbg(c, "y0", y_tok[:, 0:4 * PH])
            dbg(c, "y_tok", y_tok[:, :])
            dbg(c, "Hbar", Hbar[:, :])

            b.wait_ev("vector", ev_inproj_s)
            vop(lambda e: e.tensor_tensor(out=yg[:, :], in0=y_tok[:, :],
                                          in1=siluz[:, :], op=ALU.mult))
            ev_yg = b.mark("vector")
            b.wait_ev("scalar", ev_yg)
            sop(lambda e: e.activation(out=ynf[:, :], in_=yg[:, :], func=AF.Square,
                                       accum_out=rms_ss[:, :]))
            ev_ss = b.mark("scalar")
            b.wait_ev("vector", ev_ss)
            vop(lambda e: e.tensor_scalar(out=rms_ss[:, :], in0=rms_ss[:, :],
                                          scalar1=1.0 / DI, scalar2=None,
                                          op0=ALU.mult))
            vsp()
            ev_ssm = b.mark("vector")
            b.wait_ev("scalar", ev_ssm)
            sop(lambda e: e.activation(out=rms_rs[:, :], in_=rms_ss[:, :],
                                       func=AF.Ln, bias=c_eps[:, :]))
            ssp()
            sop(lambda e: e.activation(out=rms_rs[:, :], in_=rms_rs[:, :],
                                       func=AF.Exp, scale=c_nhalf[:, :]))
            ssp()
            ssp()
            ev_rs = b.mark("scalar")
            b.wait_ev("vector", ev_rs)
            vop(lambda e: e.tensor_scalar(out=ynf[:, :], in0=yg[:, :],
                                          scalar1=rms_rs[:, 0:1], scalar2=None,
                                          op0=ALU.mult))
            ev_yn = b.mark("vector")
            dbg(c, "ynf", ynf[:, :])
            b.wait_ev("tensor", ev_yn)
            for ev in ev_pa_free:
                b.wait_ev("tensor", ev)
            for ki in range(KI):
                reg = ki % 8
                pa = PA0 if reg < 4 else PA1
                mm(pa[:, (reg % 4) * 128:(reg % 4 + 1) * 128],
                   ynf[:, ki * 128:(ki + 1) * 128], identb[:, :], True, True)
                evt = b.mark("tensor")
                b.wait_ev("vector", evt)
                vop(lambda e, ki=ki, reg=reg, pa=pa: e.tensor_copy(
                    out=yg[:, ki * 128:(ki + 1) * 128],
                    in_=pa[:, (reg % 4) * 128:(reg % 4 + 1) * 128]))
                evv = b.mark("vector")
                b.wait_ev("tensor", evv)
            ev_ytf = ("sV", b.cnt["sV"])
            dbg(c, "ygT", yg[:, :])

            b.wait_ev("vector", ev_xnf)
            if c == 0:
                vop(lambda e: e.tensor_tensor(
                    out=ap(xd_f, 1, [[KC * T, 128], [T, KC], [1, T - 1]]),
                    in0=ap(xnf, 1, [[KC * T, 128], [T, KC], [1, T - 1]]),
                    in1=ap(xnf, 0, [[KC * T, 128], [T, KC], [1, T - 1]]),
                    op=ALU.subtract))
                vop(lambda e: e.tensor_copy(
                    out=ap(xd_f, 0, [[KC * T, 128], [T, KC], [1, 1]]),
                    in_=ap(xnf, 0, [[KC * T, 128], [T, KC], [1, 1]])))
            else:
                vop(lambda e: e.tensor_tensor(
                    out=ap(xd_f, 1, [[KC * T, 128], [T, KC], [1, T - 1]]),
                    in0=ap(xnf, 1, [[KC * T, 128], [T, KC], [1, T - 1]]),
                    in1=ap(xnf, 0, [[KC * T, 128], [T, KC], [1, T - 1]]),
                    op=ALU.subtract))
                vop(lambda e: e.tensor_tensor(
                    out=ap(xd_f, 0, [[KC * T, 128], [T, KC], [1, 1]]),
                    in0=ap(xnf, 0, [[KC * T, 128], [T, KC], [1, 1]]),
                    in1=ap(xn_last, 0, [[KC, 128], [1, KC], [1, 1]]),
                    op=ALU.subtract))
            for k in range(KC):
                vop(lambda e, k=k: e.scalar_tensor_tensor(
                    out=xr_f[:, k * T:(k + 1) * T], in0=xd_f[:, k * T:(k + 1) * T],
                    scalar=mur_cols[:, k:k + 1], in1=xnf[:, k * T:(k + 1) * T],
                    op0=ALU.mult, op1=ALU.add))
                vop(lambda e, k=k: e.scalar_tensor_tensor(
                    out=xk_f[:, k * T:(k + 1) * T], in0=xd_f[:, k * T:(k + 1) * T],
                    scalar=muk_cols[:, k:k + 1], in1=xnf[:, k * T:(k + 1) * T],
                    op0=ALU.mult, op1=ALU.add))
            vop(lambda e: e.tensor_copy(
                out=xn_last[:, :],
                in_=ap(xnf, T - 1, [[KC * T, 128], [T, KC], [1, 1]])))
            ev_xrk = b.mark("vector")
            dbg(c, "xr_f", xr_f[:, :])
            dbg(c, "xk_f", xk_f[:, :])

            b.wait_ev("tensor", ev_xrk)
            wsev = [None, None]
            for kb in range(KC):
                p2 = kb % 2
                if kb >= 2:
                    b.wait_ev("gpsimd", ev_rmm[kb - 2])
                elif c > 0 or True:
                    b.wait_ev("gpsimd", ev_woutb)
                wsev[p2] = b.dma("gpsimd", "sGD", wout_s[p2][:, :],
                                 wrTc_hbm[kb * 128:(kb + 1) * 128, :])
                b.wait_ev("tensor", wsev[p2])
                mm(PA0[:, 0:512], xr_f[:, kb * T:(kb + 1) * T],
                   wout_s[p2][:, 0:512], kb == 0, kb == KC - 1)
                mm(PA1[:, 0:512], xr_f[:, kb * T:(kb + 1) * T],
                   wout_s[p2][:, 512:1024], kb == 0, kb == KC - 1)
                if kb == 0:
                    ev_rmm = []
                ev_rmm.append(b.mark("tensor"))
            r_half = [m_b, gs_b]
            b.wait_ev("scalar", ev_rmm[-1])
            b.wait_ev("scalar", ev_yevac)
            for gg, pa in enumerate((PA0, PA1)):
                sop(lambda e, gg=gg, pa=pa: e.activation(
                    out=r_half[gg][:, :],
                    in_=pa[:, 0:512],
                    func=AF.Tanh, scale=c_nhalf[:, :]))
            ev_rt = b.mark("scalar")
            b.wait_ev("vector", ev_rt)
            for gg in range(2):
                vop(lambda e, gg=gg: e.tensor_scalar(
                    out=r_half[gg][:, :], in0=r_half[gg][:, :],
                    scalar1=-0.5, scalar2=0.5, op0=ALU.mult, op1=ALU.add))
            ev_r = b.mark("vector")
            dbg(c, "r0", m_b[:, :])
            dbg(c, "r1", gs_b[:, :])

            b.wait_ev("tensor", ev_rt)
            for db in range(KC):
                p2 = db % 2
                if db >= 2:
                    b.wait_ev("gpsimd", ev_kmm[db - 2])
                else:
                    b.wait_ev("gpsimd", ev_rmm[-1])
                wsev[p2] = b.dma("gpsimd", "sGD", wout_s[p2][:, :],
                                 wkTc_hbm[db * 128:(db + 1) * 128, :])
                b.wait_ev("tensor", wsev[p2])
                pa = PA0 if db < 4 else PA1
                for kb in range(KC):
                    mm(pa[:, (db % 4) * 128:(db % 4 + 1) * 128],
                       wout_s[p2][:, kb * 128:(kb + 1) * 128],
                       xk_f[:, kb * T:(kb + 1) * T], kb == 0, kb == KC - 1)
                if db == 0:
                    ev_kmm = []
                ev_kmm.append(b.mark("tensor"))
            b.wait_ev("scalar", ev_kmm[-1])
            sop(lambda e: e.activation(out=s1b[:, :], in_=PA0[:, :], func=AF.Relu))
            sop(lambda e: e.activation(out=s2b[:, :], in_=PA1[:, :], func=AF.Relu))
            ev_kr = b.mark("scalar")
            b.wait_ev("vector", ev_kr)
            vop(lambda e: e.tensor_tensor(out=xd_f[:, 0:512], in0=s1b[:, :],
                                          in1=s1b[:, :], op=ALU.mult))
            vop(lambda e: e.tensor_tensor(out=xd_f[:, 512:1024], in0=s2b[:, :],
                                          in1=s2b[:, :], op=ALU.mult))
            ev_k2 = b.mark("vector")
            dbg(c, "k2_f", xd_f[:, :])

            b.wait_ev("tensor", ev_kr)
            b.wait_ev("tensor", ev_ytf)
            opmarks = []
            for ki in range(KI):
                p2 = ki % 2
                if ki >= 2:
                    b.wait_ev("gpsimd", opmarks[ki - 2])
                else:
                    b.wait_ev("gpsimd", ev_kmm[-1])
                wsev[p2] = b.dma("gpsimd", "sGD", wout_s[p2][:, :],
                                 woutb_hbm[ki * 128:(ki + 1) * 128, :])
                b.wait_ev("tensor", wsev[p2])
                mm(PA0[:, 0:512], yg[:, ki * 128:(ki + 1) * 128],
                   wout_s[p2][:, 0:512], ki == 0, ki == KI - 1)
                mm(PA1[:, 0:512], yg[:, ki * 128:(ki + 1) * 128],
                   wout_s[p2][:, 512:1024], ki == 0, ki == KI - 1)
                opmarks.append(b.mark("tensor"))
            ev_attn = opmarks[-1]

            b.wait_ev("tensor", ev_k2)
            b.wait_ev("tensor", ev_yevac)
            ev_fin = None
            for gg in range(2):
                if gg == 1:
                    b.wait_ev("tensor", ev_fin)
                for db in range(KC):
                    p2 = db % 2
                    if db >= 2:
                        b.wait_ev("gpsimd", ev_vmm[db - 2])
                    else:
                        b.wait_ev("gpsimd", opmarks[-1] if gg == 0 else ev_vmm[-1])
                    wsev[p2] = b.dma("gpsimd", "sGD", wout_s[p2][:, :],
                                     wvTc_hbm[db * 128:(db + 1) * 128, :])
                    b.wait_ev("tensor", wsev[p2])
                    mm(PC[:, 0:512], xd_f[:, db * 128:(db + 1) * 128],
                       wout_s[p2][:, gg * 512:(gg + 1) * 512],
                       db == 0, db == KC - 1)
                    if db == 0:
                        ev_vmm = []
                    ev_vmm.append(b.mark("tensor"))
                ev_vm = ev_vmm[-1]
                b.wait_ev("vector", ev_vm)
                b.wait_ev("vector", ev_attn)
                b.wait_ev("vector", ev_r)
                pa = PA0 if gg == 0 else PA1
                vop(lambda e, gg=gg, pa=pa: e.scalar_tensor_tensor(
                    out=xfeat[:, gg * 512:(gg + 1) * 512],
                    in0=pa[:, 0:512], scalar=1.0,
                    in1=xC[:, gg * 512:(gg + 1) * 512],
                    op0=ALU.mult, op1=ALU.add))
                vop(lambda e, gg=gg: e.tensor_tensor(
                    out=xfeat[:, (1 - gg) * 512:(2 - gg) * 512],
                    in0=r_half[gg][:, :], in1=PC[:, 0:512],
                    op=ALU.mult))
                vop(lambda e, gg=gg: e.tensor_tensor(
                    out=xC[:, gg * 512:(gg + 1) * 512],
                    in0=xfeat[:, gg * 512:(gg + 1) * 512],
                    in1=xfeat[:, (1 - gg) * 512:(2 - gg) * 512], op=ALU.add))
                ev_fin = b.mark("vector")
            dbg(c, "out_tok", xC[:, :])
            ev_fin_prev = ev_fin
            ev_xuse_prev = ev_fin
            ev_pa_free = [ev_fin]
            b.wait_ev("sync", ev_fin)
            ev_out_dma = b.dma("sync", "sO", out_hbm[c * T:(c + 1) * T, :],
                               xC[:, :])

        b.wait_ev("sync", ev_out_dma)
        for qn in ("sD", "sGD", "sO"):
            b.queue_barrier("sync", qn)

        if DBG_HOOK is not None:
            DBG_HOOK(nc, b, -1, "__flush__", None)

        @block.tensor
        def _(e):
            for fn in b.q["tensor"]:
                fn(e)

        @block.vector
        def _(e):
            for fn in b.q["vector"]:
                fn(e)

        @block.scalar
        def _(e):
            for fn in b.q["scalar"]:
                fn(e)

        @block.gpsimd
        def _(e):
            for fn in b.q["gpsimd"]:
                fn(e)

        @block.sync
        def _(e):
            for fn in b.q["sync"]:
                fn(e)

    return nc


def host_inputs(inputs, core_b):
    f = np.float32
    import ml_dtypes
    bf = ml_dtypes.bfloat16
    x = np.ascontiguousarray(np.asarray(inputs["L0_attn_x"][core_b], dtype=f))
    alog = np.asarray(inputs["L0_attn_A_log"], dtype=f)
    dtb = np.asarray(inputs["L0_attn_dt_bias"], dtype=f)
    dsk = np.asarray(inputs["L0_attn_D_skip"], dtype=f)
    nw = np.asarray(inputs["L0_attn_norm_weight"], dtype=f)
    mur = np.asarray(inputs["L0_ffn_mu_r"], dtype=f)
    muk = np.asarray(inputs["L0_ffn_mu_k"], dtype=f)

    inv_freq = 1.0 / 10000.0 ** (np.arange(NPAIR, dtype=np.float64) / NPAIR)
    if2 = np.zeros((128, H * NPAIR), f)
    for h in range(H):
        if2[:, h * NPAIR:(h + 1) * NPAIR] = (inv_freq / (2 * np.pi)).astype(f)[None, :]
    idrep = np.zeros((H, H * T), bf)
    for h in range(H):
        idrep[h, h * T:(h + 1) * T] = bf(1.0)
    causal = np.where(np.arange(T)[None, :] >= np.arange(128)[:, None],
                      0.0, -10000.0).astype(bf)
    wmneg = np.ascontiguousarray(np.tile(causal, (1, 4)))
    return {
        "x": x,
        "w_in": np.ascontiguousarray(np.asarray(inputs["L0_attn_W_in"], dtype=f)),
        "w_out": np.ascontiguousarray(np.asarray(inputs["L0_attn_W_out"], dtype=f)),
        "w_r": np.ascontiguousarray(np.asarray(inputs["L0_ffn_W_r"], dtype=f)),
        "w_k": np.ascontiguousarray(np.asarray(inputs["L0_ffn_W_k"], dtype=f)),
        "w_v": np.ascontiguousarray(np.asarray(inputs["L0_ffn_W_v"], dtype=f)),
        "i_alog_rep": np.ascontiguousarray(np.tile(alog[None, :], (128, 1)).astype(f)),
        "i_dtb_rep": np.ascontiguousarray(np.tile(dtb[None, :], (128, 1)).astype(f)),
        "i_dsk_rep": np.ascontiguousarray(
            np.tile(np.repeat(dsk, PH)[None, :], (128, 1)).astype(f)),
        "i_nw_cols": np.ascontiguousarray(nw.reshape(KI, 128).T).astype(f),
        "i_mur_cols": np.ascontiguousarray(mur.reshape(KC, 128).T).astype(f),
        "i_muk_cols": np.ascontiguousarray(muk.reshape(KC, 128).T).astype(f),
        "i_identb": np.eye(128, dtype=bf),
        "i_identf": np.eye(128, dtype=f),
        "i_if2pi": if2,
        "i_idrep": idrep,
        "i_wmaskneg": wmneg,
    }


def kernel(**inputs):
    nc = build_program()
    in_maps = [host_inputs(inputs, bb) for bb in range(BATCH)]
    res = run_bass_kernel_spmd(nc, in_maps, core_ids=list(range(BATCH)))
    return np.stack([np.asarray(r["out"], dtype=np.float32)
                     for r in res.results])
